# version 02
# baseline (speedup 1.0000x reference)
import numpy as np
from contextlib import ExitStack

import concourse.bass as bass
import concourse.mybir as mybir
from concourse.bass_utils import run_bass_kernel_spmd

ALU = mybir.AluOpType
AF = mybir.ActivationFunctionType
AX = mybir.AxisListType
F32 = mybir.dt.float32
BF16 = mybir.dt.bfloat16
I16 = mybir.dt.int16
I32 = mybir.dt.int32

T = 8192
D = 2048
NH = 16
DH = 128
FI = 8192
NCORE = 8
NROW = 1024
EXT0 = 32
EXT1 = 64

_C64 = np.float64(T) ** 0.5
SC_HI = float(np.float32(1.0 / _C64))
SC_LO = float(np.float32(1.0 / _C64 - np.float64(np.float32(1.0 / _C64))))


class Ctr:
    def __init__(self, es, nc, name):
        self.sem = es.enter_context(nc.semaphore(name))
        self.n = 0

    def inc(self, instr, k=1):
        self.n += k
        instr.then_inc(self.sem, k)
        return instr

    def wait(self, eng, n=None):
        n = self.n if n is None else n
        if n > 0:
            eng.wait_ge(self.sem, n)


class Slot:

    def __init__(self, es, nc, name):
        self.sem = es.enter_context(nc.semaphore(name))
        self.k = 0

    def begin(self, eng):
        if self.k:
            eng.wait_ge(self.sem, self.k)

    def dma(self, ins):
        self.k += 16
        ins.then_inc(self.sem, 16)
        return self.k

    def wait(self, eng, n=None):
        n = self.k if n is None else n
        if n:
            eng.wait_ge(self.sem, n)


class Eng:
    def __init__(self, es, nc, eng, name):
        self.e = eng
        self.c = Ctr(es, nc, name)

    def op(self, instr):
        return self.c.inc(instr)

    def pre(self):
        if self.c.n:
            self.c.wait(self.e)

    def w(self, ctr, n=None):
        ctr.wait(self.e, n)



def emit_quant_scales(V, t_ap, q8, r448_col):
    v = V.e
    P, B, _ = t_ap.shape
    for f in [
        lambda: v.tensor_reduce(q8["amax"][:P, :B], t_ap, AX.X,
                                ALU.max, apply_absolute_value=True),
        lambda: v.tensor_scalar(q8["s"][:P, :B], q8["amax"][:P, :B],
                                r448_col, 1e-30, ALU.mult, ALU.max),
        lambda: v.reciprocal(q8["rs"][:P, :B], q8["s"][:P, :B]),
        lambda: v.tensor_scalar(q8["s8b"][:P, :B], q8["s"][:P, :B], 0.125,
                                None, ALU.mult),
    ]:
        V.pre()
        V.op(f())


def emit_quant(V, t_ap, q_ap, q8):
    v = V.e
    P, B, _ = t_ap.shape
    F = B * 32
    u, pc, ipw, wf, w, rb, rb160, d2, c1, k = (
        q8["u"], q8["pc"], q8["ipw"], q8["wf"], q8["w"], q8["rb"],
        q8["rb160"], q8["d2"], q8["c1"], q8["k"])
    rs3 = q8["rs"][:P, :B].unsqueeze(2).broadcast_to((P, B, 32))
    s83 = q8["s8b"][:P, :B].unsqueeze(2).broadcast_to((P, B, 32))
    r3 = lambda tt: tt[:P, :F].rearrange("p (b k) -> p b k", k=32)
    for f in [
        lambda: v.tensor_tensor(r3(u), t_ap, rs3, ALU.mult),
        lambda: v.tensor_scalar(pc[:P, :F], u[:P, :F].bitcast(I32),
                                0x7F800000, None, ALU.bitwise_and),
        lambda: v.tensor_scalar(pc[:P, :F], pc[:P, :F], 0x3C800000, None,
                                ALU.max),
        lambda: v.tensor_scalar(ipw[:P, :F], pc[:P, :F], 0x7F800000, None,
                                ALU.bitwise_xor),
        lambda: v.tensor_tensor(wf[:P, :F], u[:P, :F],
                                ipw[:P, :F].bitcast(F32), ALU.mult),
        lambda: v.tensor_scalar(w[:P, :F], wf[:P, :F], 4.0, 160.0,
                                ALU.mult, ALU.add),
        lambda: v.tensor_scalar(rb[:P, :F], w[:P, :F].bitcast(I16), 0, None,
                                ALU.bitwise_or),
        lambda: v.tensor_scalar(rb160[:P, :F], rb[:P, :F].bitcast(BF16),
                                160.0, None, ALU.subtract),
        lambda: v.scalar_tensor_tensor(d2[:P, :F], wf[:P, :F], 4.0,
                                       rb160[:P, :F], ALU.mult,
                                       ALU.subtract),
        lambda: v.tensor_scalar(c1[:P, :F], d2[:P, :F], 0.5, None,
                                ALU.is_gt),
        lambda: v.tensor_scalar(k[:P, :F], d2[:P, :F], -0.5, None,
                                ALU.is_lt),
        lambda: v.tensor_tensor(c1[:P, :F], c1[:P, :F], k[:P, :F],
                                ALU.subtract),
        lambda: v.tensor_tensor(k[:P, :F], c1[:P, :F], rb160[:P, :F],
                                ALU.add),
        lambda: v.tensor_tensor(c1[:P, :F], k[:P, :F],
                                pc[:P, :F].bitcast(F32), ALU.mult),
        lambda: v.tensor_tensor(q_ap, r3(c1), s83, ALU.mult),
    ]:
        V.pre()
        V.op(f())


def alloc_quant_scratch(stack, nc, F, tag):
    B = F // 32
    q8 = {}
    for nm, shp, dt in [("amax", [128, B], F32), ("s", [128, B], F32),
                        ("rs", [128, B], F32), ("s8b", [128, B], BF16),
                        ("u", [128, F], F32), ("pc", [128, F], I32),
                        ("ipw", [128, F], I32), ("wf", [128, F], F32),
                        ("w", [128, F], BF16), ("rb", [128, F], I16),
                        ("rb160", [128, F], BF16), ("d2", [128, F], F32),
                        ("c1", [128, F], BF16), ("k", [128, F], BF16)]:
        q8[nm] = stack.enter_context(nc.sbuf_tensor(f"q8{tag}_{nm}", shp, dt))
    return q8


def emit_ln_tile(blk, V, S, x_ap, out_ap, st_, sq, waits):
    @blk.scalar
    def _(s_):
        for ctr, n in waits:
            ctr.wait(s_, n)
        S.pre()
        S.op(s_.activation(sq[:], x_ap, AF.Square, bias=0.0, scale=1.0,
                           accum_out=st_["sxx"][:]))

    @blk.vector
    def _(v):
        for ctr, n in waits:
            ctr.wait(v, n)
        V.pre()
        V.op(v.tensor_reduce(st_["sx"][:], x_ap, AX.X, ALU.add))
        V.w(S.c)
        V.pre()
        V.op(v.tensor_scalar(st_["mu"][:], st_["sx"][:], 1.0 / D, None,
                             ALU.mult))
        V.pre()
        V.op(v.tensor_scalar(st_["var"][:], st_["sxx"][:], 1.0 / D, None,
                             ALU.mult))
        V.pre()
        V.op(v.tensor_tensor(st_["sx"][:], st_["mu"][:], st_["mu"][:],
                             ALU.mult))
        V.pre()
        V.op(v.tensor_tensor(st_["var"][:], st_["var"][:], st_["sx"][:],
                             ALU.subtract))
        V.pre()
        V.op(v.tensor_scalar(st_["var"][:], st_["var"][:], 1e-5, None,
                             ALU.add))

    @blk.scalar
    def _(s_):
        S.w(V.c)
        S.pre()
        S.op(s_.activation(st_["sxx"][:], st_["var"][:], AF.Sqrt, bias=0.0,
                           scale=1.0))

    @blk.vector
    def _(v):
        V.w(S.c)
        V.pre()
        V.op(v.reciprocal(st_["rstd"][:], st_["sxx"][:]))
        V.pre()
        V.op(v.tensor_scalar(out_ap, x_ap, st_["mu"][:], st_["rstd"][:],
                             ALU.subtract, ALU.mult))


def build_nc(max_phase=99):
    nc = bass.Bass("TRN2", target_bir_lowering=False)

    x_full = nc.dram_tensor("x_full", [T, D], F32, kind="ExternalInput").ap()
    x_q = nc.dram_tensor("x_q", [NROW, D], F32, kind="ExternalInput").ap()
    rq = nc.dram_tensor("rq", [NROW], F32, kind="ExternalInput").ap()
    wqd = nc.dram_tensor("wqd", [D, D], F32, kind="ExternalInput").ap()
    wkd = nc.dram_tensor("wkd", [D, D], F32, kind="ExternalInput").ap()
    wvd = nc.dram_tensor("wvd", [D, D], F32, kind="ExternalInput").ap()
    wod = nc.dram_tensor("wod", [D, D], F32, kind="ExternalInput").ap()
    gwd = nc.dram_tensor("gwd", [FI, D], F32, kind="ExternalInput").ap()
    uwd = nc.dram_tensor("uwd", [FI, D], F32, kind="ExternalInput").ap()
    dwd = nc.dram_tensor("dwd", [D, FI], F32, kind="ExternalInput").ap()
    out_q = nc.dram_tensor("out_q", [NROW, D], F32,
                           kind="ExternalOutput").ap()

    lnx = nc.dram_tensor("lnx", [T, D], BF16).ap()
    lnxq = nc.dram_tensor("lnxq", [NROW, D], BF16).ap()
    ktd = nc.dram_tensor("ktd", [NH, DH, T], BF16).ap()
    vtd = nc.dram_tensor("vtd", [NH, T, DH], BF16).ap()
    qgw = nc.dram_tensor("qgw", [FI, D], BF16).ap()
    quw = nc.dram_tensor("quw", [FI, D], BF16).ap()
    qdw = nc.dram_tensor("qdw", [D, FI], BF16).ap()
    ucd = nc.dram_tensor("ucd", [NROW, D], BF16).ap()
    qx2 = nc.dram_tensor("qx2", [NROW, D], BF16).ap()
    qht = nc.dram_tensor("qht", [FI, NROW], BF16).ap()
    resd = nc.dram_tensor("resd", [NROW, D], F32).ap()

    es = ExitStack()
    sb = lambda name, shp, dt: es.enter_context(nc.sbuf_tensor(name, shp, dt))

    ones_col = sb("ones_col", [128, 1], BF16)
    ones_row = sb("ones_row", [1, 128], BF16)
    ones_rowf = sb("ones_rowf", [1, 128], F32)
    ident = sb("ident", [128, 128], BF16)
    identf = sb("identf", [128, 128], F32)
    riota = sb("riota", [128, 1], F32)
    r448 = sb("r448", [128, 1], F32)
    rq_sb = sb("rq_sb", [1, NROW], F32)

    G = Eng(es, nc, nc.gpsimd, "c_g")
    V = Eng(es, nc, nc.vector, "c_v")
    S = Eng(es, nc, nc.scalar, "c_s")
    TE = Eng(es, nc, nc.tensor, "c_t")
    _slots = {}

    def SL(name):
        if name not in _slots:
            _slots[name] = Slot(es, nc, "sl_" + name)
        return _slots[name]

    def fin(slot_names=None):
        with nc.Block() as blk:
            @blk.sync
            def _(sy):
                for sl in _slots.values():
                    sl.wait(sy)

    def pend(blk):
        @blk.sync
        def _(sy):
            for sl in _slots.values():
                sl.wait(sy)

    with nc.Block() as blk:
        @blk.sync
        def _(sy):
            SL("rq").begin(sy)
            SL("rq").dma(sy.dma_start(out=rq_sb[0:1, :], in_=rq.unsqueeze(0)))

        @blk.gpsimd
        def _(g):
            for f in [lambda: g.memset(ones_col[:], 1.0),
                      lambda: g.memset(ones_row[:], 1.0),
                      lambda: g.memset(ones_rowf[:], 1.0),
                      lambda: g.memset(r448[:], 448.0),
                      lambda: g.iota(riota[:], [[0, 1]], base=0,
                                     channel_multiplier=1,
                                     allow_small_or_imprecise_dtypes=True),
                      lambda: g.iota(identf[:], [[1, 128]], base=0,
                                     channel_multiplier=-1,
                                     allow_small_or_imprecise_dtypes=True)]:
                G.pre()
                G.op(f())

        @blk.vector
        def _(v):
            V.w(G.c)
            V.pre()
            V.op(v.tensor_scalar(ident[:], identf[:], 0.0, None,
                                 ALU.is_equal))
            V.pre()
            V.op(v.reciprocal(r448[:], r448[:]))

    if max_phase < 1:
        fin()
        es.close()
        return nc

    with ExitStack() as ph:
        pb = lambda name, shp, dt: ph.enter_context(
            nc.sbuf_tensor(name, shp, dt))
        xt = [pb(f"p1_x{i}", [128, D], F32) for i in range(2)]
        lno = [pb(f"p1_o{i}", [128, D], BF16) for i in range(2)]
        sq = pb("p1_sq", [128, D], F32)
        st_ = {nm: pb("p1_" + nm, [128, 1], F32)
               for nm in ["sx", "sxx", "mu", "var", "rstd"]}
        ntile = T // 128 + NROW // 128
        xs = [SL("p1x0"), SL("p1x1")]
        os_ = [SL("p1o0"), SL("p1o1")]
        vmark = {}

        with nc.Block() as blk:
            for i in range(ntile):
                src = (x_full[i * 128:(i + 1) * 128, :] if i < 64 else
                       x_q[(i - 64) * 128:(i - 63) * 128, :])
                dst = (lnx[i * 128:(i + 1) * 128, :] if i < 64 else
                       lnxq[(i - 64) * 128:(i - 63) * 128, :])

                @blk.sync
                def _(sy, i=i, src=src):
                    if i >= 2:
                        V.c.wait(sy, vmark[i - 2])
                    xs[i % 2].begin(sy)
                    xs[i % 2].dma(sy.dma_start(out=xt[i % 2][:], in_=src))

                lnw = [(xs[i % 2], xs[i % 2].k),
                       (os_[i % 2], os_[i % 2].k)]
                if i >= 1:
                    lnw.append((V.c, vmark[i - 1]))
                emit_ln_tile(blk, V, S, xt[i % 2][:], lno[i % 2][:], st_, sq,
                             waits=lnw)
                vmark[i] = V.c.n

                @blk.gpsimd
                def _(g, i=i, dst=dst):
                    G.w(V.c)
                    os_[i % 2].begin(g)
                    os_[i % 2].dma(g.dma_start(out=dst, in_=lno[i % 2][:]))

            pend(blk)

    if max_phase < 2:
        fin()
        es.close()
        return nc

    with ExitStack() as ph:
        pb = lambda name, shp, dt: ph.enter_context(
            nc.sbuf_tensor(name, shp, dt))
        wt = [pb(f"p2_w{i}", [128, D], F32) for i in range(2)]
        qt = [pb(f"p2_q{i}", [128, D], BF16) for i in range(2)]
        q8 = alloc_quant_scratch(ph, nc, D, "w")
        srcs = []
        for i in range(FI // 128):
            srcs.append((gwd[i * 128:(i + 1) * 128, :],
                         qgw[i * 128:(i + 1) * 128, :]))
        for i in range(FI // 128):
            srcs.append((uwd[i * 128:(i + 1) * 128, :],
                         quw[i * 128:(i + 1) * 128, :]))
        for i in range(D // 128):
            for j in range(FI // D):
                srcs.append((dwd[i * 128:(i + 1) * 128, j * D:(j + 1) * D],
                             qdw[i * 128:(i + 1) * 128, j * D:(j + 1) * D]))
        ws = [SL("p2w0"), SL("p2w1")]
        qs = [SL("p2q0"), SL("p2q1")]
        vmark = {}

        with nc.Block() as blk:
            for i, (src, dst) in enumerate(srcs):
                @blk.gpsimd
                def _(g, i=i, src=src):
                    if i >= 2:
                        V.c.wait(g, vmark[i - 2])
                    ws[i % 2].begin(g)
                    ws[i % 2].dma(g.dma_start(out=wt[i % 2][:], in_=src))

                @blk.vector
                def _(v, i=i):
                    ws[i % 2].wait(v)
                    qs[i % 2].wait(v)
                    t3 = wt[i % 2][:].rearrange("p (b k) -> p b k", k=32)
                    q3 = qt[i % 2][:].rearrange("p (b k) -> p b k", k=32)
                    emit_quant_scales(V, t3, q8, r448[:, 0:1])
                    emit_quant(V, t3, q3, q8)
                    vmark[i] = V.c.n

                @blk.sync
                def _(sy, i=i, dst=dst):
                    V.c.wait(sy, vmark[i])
                    qs[i % 2].begin(sy)
                    qs[i % 2].dma(sy.dma_start(out=dst, in_=qt[i % 2][:]))

            pend(blk)

    if max_phase < 3:
        fin()
        es.close()
        return nc

    with ExitStack() as ph:
        pb = lambda name, shp, dt: ph.enter_context(
            nc.sbuf_tensor(name, shp, dt))
        pp = lambda name, shp: ph.enter_context(
            nc.psum_tensor(name, shp, F32))
        wkb = pb("p3_wkb", [128, 16 * D], BF16)
        wvb = pb("p3_wvb", [128, 16 * D], BF16)
        wfp = [pb(f"p3_wf{i}", [128, D], F32) for i in range(2)]
        tmp = pb("p3_tmp", [128, D], F32)
        lxt = pb("p3_lxt", [128, 16 * 512], BF16)
        kev = [pb(f"p3_kev{i}", [128, 512], BF16) for i in range(2)]
        vev = [pb(f"p3_vev{i}", [128, 512], BF16) for i in range(2)]
        kps = [pp(f"p3_kps{i}", [128, 512]) for i in range(2)]
        vps = [pp(f"p3_vps{i}", [128, 512]) for i in range(2)]
        wls = [SL("p3w0"), SL("p3w1")]
        lxs = SL("p3lx")
        kst = [SL("p3k0"), SL("p3k1")]
        vst = [SL("p3v0"), SL("p3v1")]
        vm = {}
        evk = {}
        evv = {}
        kmk = {}
        vmk = {}
        te0 = TE.c.n

        with nc.Block() as blk:
            for i in range(32):
                @blk.sync
                def _(sy, i=i):
                    if i >= 2:
                        V.c.wait(sy, vm[i - 2])
                    src = (wkd if i < 16 else wvd)
                    j = i % 16
                    wls[i % 2].begin(sy)
                    wls[i % 2].dma(sy.dma_start(
                        out=wfp[i % 2][:], in_=src[j * 128:(j + 1) * 128, :]))

                @blk.vector
                def _(v, i=i):
                    wls[i % 2].wait(v)
                    j = i % 16
                    if i < 16:
                        V.pre()
                        V.op(v.tensor_scalar(tmp[:], wfp[i % 2][:], SC_LO,
                                             None, ALU.mult))
                        V.pre()
                        V.op(v.scalar_tensor_tensor(
                            wkb[:, j * D:(j + 1) * D], wfp[i % 2][:], SC_HI,
                            tmp[:], ALU.mult, ALU.add))
                    else:
                        V.pre()
                        V.op(v.tensor_copy(wvb[:, j * D:(j + 1) * D],
                                           wfp[i % 2][:]))
                    vm[i] = V.c.n

            for ch in range(16):
                @blk.sync
                def _(sy, ch=ch):
                    if ch >= 1:
                        TE.c.wait(sy, vmk[16 * ch - 1])
                    lxs.begin(sy)
                    for dt_ in range(16):
                        src = lnx[ch * 512:(ch + 1) * 512,
                                  dt_ * 128:(dt_ + 1) * 128]
                        lxs.dma(sy.dma_start_transpose(
                            lxt[:, dt_ * 512:(dt_ + 1) * 512], src))

                for h in range(16):
                    @blk.tensor
                    def _(t, ch=ch, h=h):
                        g = ch * 16 + h
                        if h == 0:
                            lxs.wait(t)
                            V.c.wait(t, vm[31])
                        if g >= 2:
                            V.c.wait(t, evk[g - 2])
                        for dt_ in range(16):
                            ins = t.matmul(kps[g % 2][:],
                                           wkb[:, dt_ * D + h * 128:
                                               dt_ * D + (h + 1) * 128],
                                           lxt[:, dt_ * 512:
                                               (dt_ + 1) * 512],
                                           start=(dt_ == 0),
                                           stop=(dt_ == 15))
                        kmk[g] = TE.c.n + 1
                        TE.op(ins)

                    @blk.vector
                    def _(v, ch=ch, h=h):
                        g = ch * 16 + h
                        TE.c.wait(v, kmk[g])
                        kst[g % 2].wait(v)
                        V.pre()
                        V.op(v.tensor_copy(kev[g % 2][:], kps[g % 2][:]))
                        evk[g] = V.c.n

                    @blk.gpsimd
                    def _(g_, ch=ch, h=h):
                        g = ch * 16 + h
                        V.c.wait(g_, evk[g])
                        kst[g % 2].begin(g_)
                        kst[g % 2].dma(g_.dma_start(
                            out=ktd[h, :, ch * 512:(ch + 1) * 512],
                            in_=kev[g % 2][:]))

                for idx in range(16):
                    @blk.tensor
                    def _(t, ch=ch, idx=idx):
                        g = ch * 16 + idx
                        if g >= 2:
                            V.c.wait(t, evv[g - 2])
                        r, q4 = idx // 4, idx % 4
                        for dt_ in range(16):
                            ins = t.matmul(
                                vps[g % 2][:],
                                lxt[:, dt_ * 512 + r * 128:
                                    dt_ * 512 + (r + 1) * 128],
                                wvb[:, dt_ * D + q4 * 512:
                                    dt_ * D + (q4 + 1) * 512],
                                start=(dt_ == 0), stop=(dt_ == 15))
                        vmk[g] = TE.c.n + 1
                        TE.op(ins)

                    @blk.vector
                    def _(v, ch=ch, idx=idx):
                        g = ch * 16 + idx
                        TE.c.wait(v, vmk[g])
                        vst[g % 2].wait(v)
                        V.pre()
                        V.op(v.tensor_copy(vev[g % 2][:], vps[g % 2][:]))
                        evv[g] = V.c.n

                    @blk.gpsimd
                    def _(g_, ch=ch, idx=idx):
                        g = ch * 16 + idx
                        V.c.wait(g_, evv[g])
                        r, q4 = idx // 4, idx % 4
                        vst[g % 2].begin(g_)
                        for hh in range(4):
                            h = q4 * 4 + hh
                            vst[g % 2].dma(g_.dma_start(
                                out=vtd[h, ch * 512 + r * 128:
                                        ch * 512 + (r + 1) * 128, :],
                                in_=vev[g % 2][:, hh * 128:(hh + 1) * 128]))

            pend(blk)

    if max_phase < 4:
        fin()
        es.close()
        return nc

    es_uc = ExitStack()
    ucat = es_uc.enter_context(nc.sbuf_tensor("ucat", [128, 8 * D], BF16))
    es_qm = ExitStack()
    qts = es_qm.enter_context(
        nc.sbuf_tensor("qts", [128, NH * NROW], BF16))
    with ExitStack() as ph:
        pb = lambda name, shp, dt: ph.enter_context(
            nc.sbuf_tensor(name, shp, dt))
        pp = lambda name, shp: ph.enter_context(
            nc.psum_tensor(name, shp, F32))
        wqb = pb("p4_wqb", [128, 16 * D], BF16)
        lxq = pb("p4_lxq", [128, 16 * 512], BF16)
        qps = [pp(f"p4_qps{i}", [128, 512]) for i in range(2)]
        te0 = TE.c.n
        wqs = SL("p4wq")
        lqs = SL("p4lx")
        ev = {}
        qmk = {}

        with nc.Block() as blk:
            @blk.gpsimd
            def _(g):
                wqs.begin(g)
                for j in range(16):
                    wqs.dma(g.dma_start(out=wqb[:, j * D:(j + 1) * D],
                                        in_=wqd[j * 128:(j + 1) * 128, :]))

            for oc in range(2):
                @blk.sync
                def _(sy, oc=oc):
                    if oc >= 1:
                        TE.c.wait(sy, qmk[15])
                    lqs.begin(sy)
                    for dt_ in range(16):
                        src = lnxq[oc * 512:(oc + 1) * 512,
                                   dt_ * 128:(dt_ + 1) * 128]
                        lqs.dma(sy.dma_start_transpose(
                            lxq[:, dt_ * 512:(dt_ + 1) * 512], src))

                for h in range(16):
                    @blk.tensor
                    def _(t, oc=oc, h=h):
                        g = oc * 16 + h
                        if h == 0:
                            lqs.wait(t)
                            wqs.wait(t)
                        if g >= 2:
                            V.c.wait(t, ev[g - 2])
                        for dt_ in range(16):
                            ins = t.matmul(qps[g % 2][:],
                                           wqb[:, dt_ * D + h * 128:
                                               dt_ * D + (h + 1) * 128],
                                           lxq[:, dt_ * 512:
                                               (dt_ + 1) * 512],
                                           start=(dt_ == 0),
                                           stop=(dt_ == 15))
                        qmk[g] = TE.c.n + 1
                        TE.op(ins)

                    @blk.vector
                    def _(v, oc=oc, h=h):
                        g = oc * 16 + h
                        TE.c.wait(v, qmk[g])
                        V.pre()
                        V.op(v.tensor_copy(
                            qts[:, h * NROW + oc * 512:
                                h * NROW + (oc + 1) * 512],
                            qps[g % 2][:]))
                        ev[g] = V.c.n

            pend(blk)

    if max_phase < 5:
        fin()
        es.close()
        return nc

    masks = es_qm.enter_context(
        nc.sbuf_tensor("masks", [128, 64 * 512], BF16))
    with ExitStack() as ph:
        pb = lambda name, shp, dt: ph.enter_context(
            nc.sbuf_tensor(name, shp, dt))
        khT = pb("p5_khT", [128, T], BF16)
        vh = pb("p5_vh", [128, 64 * 129], BF16)
        expT = [pb(f"p5_e{i}", [128, 512], BF16) for i in range(2)]
        ssuf = pb("p5_ssuf", [1, 129], BF16)
        zr = pb("p5_zr", [128, 1], F32)
        rqps = ph.enter_context(nc.psum_tensor("p5_rqps", [128, 512], F32))
        scps = [ph.enter_context(nc.psum_tensor(f"p5_sc{i}", [128, 512], F32))
                for i in range(2)]
        ups = [ph.enter_context(nc.psum_tensor(f"p5_up{i}", [128, 129], F32))
               for i in range(4)]
        sfps = ph.enter_context(nc.psum_tensor("p5_sfps", [1, 129], F32))
        khs = SL("p5kh")
        vhs = SL("p5vh")
        tiles = [(0, j) for j in range(EXT0)] + [(1, j) for j in range(EXT1)]
        NTI = len(tiles)

        evm = {}
        ev0m = {}
        sfm = {}
        scm = {}
        avm = {}
        corm = {}
        expm = {}

        with nc.Block() as blk:
            for b in range(2):
                @blk.tensor
                def _(t, b=b):
                    SL("rq").wait(t)
                    V.c.wait(t)
                    ins = t.matmul(rqps[:], ones_rowf[:],
                                   rq_sb[:, b * 512:(b + 1) * 512],
                                   start=True, stop=True)
                    TE.op(ins)

                @blk.vector
                def _(v, b=b):
                    V.w(TE.c)
                    for jl in range(32):
                        gj = b * 32 + jl
                        V.pre()
                        V.op(v.tensor_scalar(
                            masks[:, gj * 512:(gj + 1) * 512], rqps[:],
                            riota[:, 0:1], float(128 * gj), ALU.subtract,
                            ALU.is_ge))

            @blk.gpsimd
            def _(g):
                G.pre()
                G.op(g.memset(
                    vh[:].rearrange("p (b c) -> p b c", c=129)[:, :, 128:129],
                    1.0))

            for h in range(NH):
                @blk.sync
                def _(sy, h=h):
                    if h >= 1:
                        V.c.wait(sy, evm[h - 1])
                        S.c.wait(sy, expm[(h - 1, NTI - 1)])
                    khs.begin(sy)
                    khs.dma(sy.dma_start(out=khT[:], in_=ktd[h]))
                    vhs.begin(sy)
                    vhs.dma(sy.dma_start(
                        out=vh[:].rearrange("p (b c) -> p b c",
                                            c=129)[:, :, 0:128],
                        in_=vtd[h].rearrange("(b p) d -> p b d", p=128)))

                @blk.tensor
                def _(t, h=h):
                    khs.wait(t)
                    vhs.wait(t)
                    G.c.wait(t)
                    if h >= 1:
                        V.c.wait(t, sfm[h - 1])
                    for j in range(EXT0, 64):
                        ins = t.matmul(sfps[:], ones_col[:],
                                       vh[:, j * 129:(j + 1) * 129],
                                       start=(j == EXT0), stop=(j == 63))
                    TE.op(ins)

                @blk.vector
                def _(v, h=h):
                    V.w(TE.c)
                    if h >= 1:
                        TE.c.wait(v, corm[h - 1])
                    V.pre()
                    V.op(v.tensor_copy(ssuf[:], sfps[:]))
                    sfm[h] = V.c.n

                @blk.tensor
                def _(t, h=h):
                    if h >= 1:
                        S.c.wait(t, expm[(h - 1, NTI - 1)])
                    b0, j0 = tiles[0]
                    ins = t.matmul(scps[0][:],
                                   khT[:, j0 * 128:(j0 + 1) * 128],
                                   qts[:, h * NROW + b0 * 512:
                                       h * NROW + (b0 + 1) * 512],
                                   start=True, stop=True)
                    scm[(h, 0)] = TE.c.n + 1
                    TE.op(ins)

                for idx in range(NTI):
                    b, j = tiles[idx]
                    masked = (b == 0) or (j >= 32)

                    @blk.vector
                    def _(v, h=h, idx=idx, masked=masked, j=j):
                        if masked:
                            TE.c.wait(v, scm[(h, idx)])
                            V.pre()
                            V.op(v.tensor_tensor(
                                scps[idx % 2][:], scps[idx % 2][:],
                                masks[:, j * 512:(j + 1) * 512], ALU.mult))

                    @blk.scalar
                    def _(s_, h=h, idx=idx, masked=masked):
                        if masked:
                            S.w(V.c)
                        else:
                            TE.c.wait(s_, scm[(h, idx)])
                        S.pre()
                        S.op(s_.activation(expT[idx % 2][:],
                                           scps[idx % 2][:], AF.Exp,
                                           bias=0.0, scale=1.0))
                        expm[(h, idx)] = S.c.n

                    @blk.tensor
                    def _(t, h=h, idx=idx):
                        if idx + 1 < NTI:
                            bn, jn = tiles[idx + 1]
                            if idx >= 1:
                                S.c.wait(t, expm[(h, idx - 1)])
                            ins = t.matmul(
                                scps[(idx + 1) % 2][:],
                                khT[:, jn * 128:(jn + 1) * 128],
                                qts[:, h * NROW + bn * 512:
                                    h * NROW + (bn + 1) * 512],
                                start=True, stop=True)
                            scm[(h, idx + 1)] = TE.c.n + 1
                            TE.op(ins)
                        b, j = tiles[idx]
                        S.c.wait(t, expm[(h, idx)])
                        if idx == 0 and h >= 1:
                            V.c.wait(t, evm[h - 1])
                        if idx == EXT0:
                            V.c.wait(t, ev0m[h])
                        for q in range(4):
                            ins = t.matmul(ups[q][:],
                                           expT[idx % 2][:, q * 128:
                                                          (q + 1) * 128],
                                           vh[:, j * 129:(j + 1) * 129],
                                           start=(j == 0),
                                           stop=(b == 1 and
                                                 j == EXT1 - 1))
                        avm[(h, idx)] = TE.c.n + 1
                        TE.op(ins)
                        if (b, j) == (0, EXT0 - 1):
                            V.c.wait(t, sfm[h])
                            for q in range(4):
                                ins = t.matmul(ups[q][:], ones_row[:],
                                               ssuf[:], start=False,
                                               stop=True)
                            corm[h] = TE.c.n + 1
                            TE.op(ins)

                    if (b, j) in ((0, EXT0 - 1), (1, EXT1 - 1)):
                        @blk.vector
                        def _(v, h=h, b=b, idx=idx):
                            TE.c.wait(v, corm[h] if b == 0 else avm[(h, idx)])
                            for q in range(4):
                                tt = b * 4 + q
                                V.pre()
                                V.op(v.reciprocal(zr[:], ups[q][:, 128:129]))
                                V.pre()
                                V.op(v.tensor_scalar(
                                    ucat[:, tt * D + h * 128:
                                         tt * D + (h + 1) * 128],
                                    ups[q][:, 0:128], zr[:, 0:1], None,
                                    ALU.mult))
                            if b == 1:
                                evm[h] = V.c.n
                            else:
                                ev0m[h] = V.c.n

            pend(blk)

    es_qm.close()

    if max_phase < 6:
        with nc.Block() as blk:
            @blk.sync
            def _(sy):
                V.c.wait(sy)
                uds = SL("ucdump")
                uds.begin(sy)
                for tt in range(8):
                    uds.dma(sy.dma_start(
                        out=ucd[tt * 128:(tt + 1) * 128, :],
                        in_=ucat[:, tt * D:(tt + 1) * D]))
                uds.wait(sy)
        es.close()
        return nc

    with ExitStack() as ph:
        pb = lambda name, shp, dt: ph.enter_context(
            nc.sbuf_tensor(name, shp, dt))
        wob = pb("p6_wob", [128, 16 * D], BF16)
        uct = pb("p6_uct", [128, 16 * 128], BF16)
        xqt = [pb(f"p6_xq{i}", [128, D], F32) for i in range(2)]
        res = pb("p6_res", [128, D], F32)
        ln2o = pb("p6_ln2o", [128, D], F32)
        qx2t = pb("p6_qx2t", [128, D], BF16)
        sq6 = pb("p6_sq", [128, D], F32)
        st6 = {nm: pb("p6_" + nm, [128, 1], F32)
               for nm in ["sx", "sxx", "mu", "var", "rstd"]}
        q86 = alloc_quant_scratch(ph, nc, 512, "x")
        tpps = [ph.enter_context(
            nc.psum_tensor(f"p6_tp{i}", [128, 128], BF16)) for i in range(2)]
        ywps = [ph.enter_context(
            nc.psum_tensor(f"p6_yw{i}", [128, 512], F32)) for i in range(2)]
        wos = SL("p6wo")
        xqs = [SL("p6x0"), SL("p6x1")]
        rss = SL("p6res")
        q2s = SL("p6qx2")
        tr_m = {}
        tr_evm = {}
        yw_m = {}
        yw_evm = {}

        with nc.Block() as blk:
            @blk.gpsimd
            def _(g):
                wos.begin(g)
                for j in range(16):
                    wos.dma(g.dma_start(out=wob[:, j * D:(j + 1) * D],
                                        in_=wod[j * 128:(j + 1) * 128, :]))

            for tt in range(8):
                @blk.sync
                def _(sy, tt=tt):
                    if tt >= 2:
                        V.c.wait(sy, yw_evm[(tt - 2) * 4 + 3])
                    xqs[tt % 2].begin(sy)
                    xqs[tt % 2].dma(sy.dma_start(
                        out=xqt[tt % 2][:],
                        in_=x_q[tt * 128:(tt + 1) * 128, :]))

                for hh in range(16):
                    @blk.tensor
                    def _(t, tt=tt, hh=hh):
                        if hh == 0:
                            V.c.wait(t)
                            wos.wait(t)
                        if (tt, hh) > (0, 1):
                            prev = ((tt, hh - 2) if hh >= 2 else
                                    (tt - 1, 14 + hh))
                            V.c.wait(t, tr_evm[prev])
                        ins = t.transpose(tpps[hh % 2][:],
                                          ucat[:, tt * D + hh * 128:
                                               tt * D + (hh + 1) * 128],
                                          ident[:])
                        tr_m[(tt, hh)] = TE.c.n + 1
                        TE.op(ins)

                    @blk.vector
                    def _(v, tt=tt, hh=hh):
                        TE.c.wait(v, tr_m[(tt, hh)])
                        V.pre()
                        V.op(v.tensor_copy(uct[:, hh * 128:(hh + 1) * 128],
                                           tpps[hh % 2][:]))
                        tr_evm[(tt, hh)] = V.c.n

                for o4 in range(4):
                    @blk.tensor
                    def _(t, tt=tt, o4=o4):
                        if o4 == 0:
                            V.c.wait(t)
                        if tt * 4 + o4 >= 2:
                            V.c.wait(t, yw_evm[tt * 4 + o4 - 2])
                        for dt_ in range(16):
                            ins = t.matmul(ywps[o4 % 2][:],
                                           uct[:, dt_ * 128:
                                               (dt_ + 1) * 128],
                                           wob[:, dt_ * D + o4 * 512:
                                               dt_ * D + (o4 + 1) * 512],
                                           start=(dt_ == 0),
                                           stop=(dt_ == 15))
                        yw_m[tt * 4 + o4] = TE.c.n + 1
                        TE.op(ins)

                    @blk.vector
                    def _(v, tt=tt, o4=o4):
                        if o4 == 0:
                            xqs[tt % 2].wait(v)
                            V.w(S.c)
                            rss.wait(v)
                        TE.c.wait(v, yw_m[tt * 4 + o4])
                        V.pre()
                        V.op(v.tensor_tensor(
                            res[:, o4 * 512:(o4 + 1) * 512],
                            xqt[tt % 2][:, o4 * 512:(o4 + 1) * 512],
                            ywps[o4 % 2][:], ALU.add))
                        yw_evm[tt * 4 + o4] = V.c.n

                @blk.gpsimd
                def _(g, tt=tt):
                    V.c.wait(g, yw_evm[tt * 4 + 3])
                    rss.begin(g)
                    rss.dma(g.dma_start(out=resd[tt * 128:(tt + 1) * 128, :],
                                        in_=res[:]))

                emit_ln_tile(blk, V, S, res[:], ln2o[:], st6, sq6,
                             waits=[(V.c, yw_evm[tt * 4 + 3])])

                @blk.vector
                def _(v, tt=tt):
                    q2s.wait(v)
                    for cq in range(4):
                        t3 = ln2o[:, cq * 512:(cq + 1) * 512].rearrange(
                            "p (b k) -> p b k", k=32)
                        q3 = qx2t[:, cq * 512:(cq + 1) * 512].rearrange(
                            "p (b k) -> p b k", k=32)
                        emit_quant_scales(V, t3, q86, r448[:, 0:1])
                        emit_quant(V, t3, q3, q86)

                @blk.sync
                def _(sy, tt=tt):
                    V.c.wait(sy)
                    q2s.begin(sy)
                    q2s.dma(sy.dma_start(out=qx2[tt * 128:(tt + 1) * 128, :],
                                         in_=qx2t[:]))

            pend(blk)

    es_uc.close()

    if max_phase < 7:
        fin()
        es.close()
        return nc

    with ExitStack() as ph:
        pb = lambda name, shp, dt: ph.enter_context(
            nc.sbuf_tensor(name, shp, dt))
        qx2T = pb("p7_qx2T", [128, 16 * NROW], BF16)
        gwt = pb("p7_gwt", [128, 16 * 512], BF16)
        uwt = pb("p7_uwt", [128, 16 * 512], BF16)
        gsl = pb("p7_gsl", [128, 512], BF16)
        hbt = pb("p7_hbt", [128, 512], BF16)
        qhb = pb("p7_qhb", [128, 512], BF16)
        qhtt = [pb(f"p7_qhtt{i}", [128, 128], BF16) for i in range(2)]
        q87 = alloc_quant_scratch(ph, nc, 512, "h")
        gps = ph.enter_context(nc.psum_tensor("p7_gps", [128, 512], F32))
        upsn = ph.enter_context(nc.psum_tensor("p7_ups", [128, 512], F32))
        tp7 = [ph.enter_context(
            nc.psum_tensor(f"p7_tp{i}", [128, 128], BF16)) for i in range(2)]
        x2s = SL("p7x2")
        gws = SL("p7gw")
        uws = SL("p7uw")
        qhs = [SL("p7qh0"), SL("p7qh1")]
        tp7_m = {}
        tp7_evm = {}
        gmk = {}
        umk = {}

        with nc.Block() as blk:
            @blk.sync
            def _(sy):
                SL("p6qx2").wait(sy)
                x2s.begin(sy)
                for dt_ in range(16):
                    for ttq in range(8):
                        src = qx2[ttq * 128:(ttq + 1) * 128,
                                  dt_ * 128:(dt_ + 1) * 128]
                        x2s.dma(sy.dma_start_transpose(
                            qx2T[:, dt_ * NROW + ttq * 128:
                                 dt_ * NROW + (ttq + 1) * 128], src))

            for ic in range(16):
                @blk.sync
                def _(sy, ic=ic):
                    if ic >= 1:
                        TE.c.wait(sy)
                    gws.begin(sy)
                    uws.begin(sy)
                    for dt_ in range(16):
                        gws.dma(sy.dma_start_transpose(
                            gwt[:, dt_ * 512:(dt_ + 1) * 512],
                            qgw[ic * 512:(ic + 1) * 512,
                                dt_ * 128:(dt_ + 1) * 128]))
                        uws.dma(sy.dma_start_transpose(
                            uwt[:, dt_ * 512:(dt_ + 1) * 512],
                            quw[ic * 512:(ic + 1) * 512,
                                dt_ * 128:(dt_ + 1) * 128]))

                for ttq in range(8):
                    @blk.tensor
                    def _(t, ic=ic, ttq=ttq):
                        gws.wait(t)
                        uws.wait(t)
                        x2s.wait(t)
                        if ic * 8 + ttq >= 1:
                            V.c.wait(t)
                            S.c.wait(t)
                        for dt_ in range(16):
                            gin = t.matmul(gps[:],
                                           qx2T[:, dt_ * NROW + ttq * 128:
                                                dt_ * NROW +
                                                (ttq + 1) * 128],
                                           gwt[:, dt_ * 512:
                                               (dt_ + 1) * 512],
                                           start=(dt_ == 0),
                                           stop=(dt_ == 15))
                        gmk[ic * 8 + ttq] = TE.c.n + 1
                        TE.op(gin)
                        for dt_ in range(16):
                            uin = t.matmul(upsn[:],
                                           qx2T[:, dt_ * NROW + ttq * 128:
                                                dt_ * NROW +
                                                (ttq + 1) * 128],
                                           uwt[:, dt_ * 512:
                                               (dt_ + 1) * 512],
                                           start=(dt_ == 0),
                                           stop=(dt_ == 15))
                        umk[ic * 8 + ttq] = TE.c.n + 1
                        TE.op(uin)

                    @blk.scalar
                    def _(s_, ic=ic, ttq=ttq):
                        TE.c.wait(s_, gmk[ic * 8 + ttq])
                        S.w(V.c)
                        S.pre()
                        S.op(s_.activation(gsl[:], gps[:], AF.Silu,
                                           bias=0.0, scale=1.0))

                    @blk.vector
                    def _(v, ic=ic, ttq=ttq):
                        V.w(S.c)
                        TE.c.wait(v, umk[ic * 8 + ttq])
                        V.pre()
                        V.op(v.tensor_tensor(hbt[:], gsl[:], upsn[:],
                                             ALU.mult))
                        t3 = hbt[:].rearrange("p (b k) -> p b k", k=32)
                        q3 = qhb[:].rearrange("p (b k) -> p b k", k=32)
                        emit_quant_scales(V, t3, q87, r448[:, 0:1])
                        emit_quant(V, t3, q3, q87)

                    for q in range(4):
                        @blk.tensor
                        def _(t, ic=ic, ttq=ttq, q=q):
                            gidx = ic * 32 + ttq * 4 + q
                            if q == 0:
                                V.c.wait(t)
                            if gidx >= 2:
                                V.c.wait(t, tp7_evm[gidx - 2])
                            ins = t.transpose(tp7[q % 2][:],
                                              qhb[:, q * 128:
                                                  (q + 1) * 128],
                                              ident[:])
                            tp7_m[gidx] = TE.c.n + 1
                            TE.op(ins)

                        @blk.vector
                        def _(v, ic=ic, ttq=ttq, q=q):
                            gidx = ic * 32 + ttq * 4 + q
                            TE.c.wait(v, tp7_m[gidx])
                            qhs[q % 2].wait(v)
                            V.pre()
                            V.op(v.tensor_copy(qhtt[q % 2][:],
                                               tp7[q % 2][:]))
                            tp7_evm[gidx] = V.c.n

                        @blk.sync
                        def _(sy, ic=ic, ttq=ttq, q=q):
                            gidx = ic * 32 + ttq * 4 + q
                            V.c.wait(sy, tp7_evm[gidx])
                            qhs[q % 2].begin(sy)
                            qhs[q % 2].dma(sy.dma_start(
                                out=qht[ic * 512 + q * 128:
                                        ic * 512 + (q + 1) * 128,
                                        ttq * 128:(ttq + 1) * 128],
                                in_=qhtt[q % 2][:]))

            pend(blk)

    if max_phase < 8:
        fin()
        es.close()
        return nc

    with ExitStack() as ph:
        pb = lambda name, shp, dt: ph.enter_context(
            nc.sbuf_tensor(name, shp, dt))
        dwt = pb("p8_dwt", [128, 64 * 512], BF16)
        qhr = [pb(f"p8_qhr{i}", [128, NROW], BF16) for i in range(2)]
        rsl = [pb(f"p8_rsl{i}", [128, 512], F32) for i in range(2)]
        ot = [pb(f"p8_ot{i}", [128, 512], F32) for i in range(2)]
        dps = [ph.enter_context(
            nc.psum_tensor(f"p8_dp{i}", [128, 512], F32)) for i in range(8)]
        dws = SL("p8dw")
        qrs = [SL("p8q0"), SL("p8q1")]
        rls = [SL("p8r0"), SL("p8r1")]
        ous = [SL("p8o0"), SL("p8o1")]
        dn_m = {}
        ev_m = {}

        with nc.Block() as blk:
            @blk.sync
            def _(sy):
                for s4 in ["p7qh0", "p7qh1", "p6res"]:
                    SL(s4).wait(sy)

            for oc in range(4):
                @blk.sync
                def _(sy, oc=oc):
                    if oc >= 1:
                        TE.c.wait(sy)
                    dws.begin(sy)
                    for ii in range(64):
                        dws.dma(sy.dma_start_transpose(
                            dwt[:, ii * 512:(ii + 1) * 512],
                            qdw[oc * 512:(oc + 1) * 512,
                                ii * 128:(ii + 1) * 128]))

                for ii in range(64):
                    @blk.gpsimd
                    def _(g, oc=oc, ii=ii):
                        if (oc, ii) > (0, 1):
                            prev = ((oc, ii - 2) if ii >= 2 else
                                    (oc - 1, 62 + ii))
                            TE.c.wait(g, dn_m[prev])
                        qrs[ii % 2].begin(g)
                        qrs[ii % 2].dma(g.dma_start(
                            out=qhr[ii % 2][:],
                            in_=qht[ii * 128:(ii + 1) * 128, :]))

                    @blk.tensor
                    def _(t, oc=oc, ii=ii):
                        if ii == 0:
                            dws.wait(t)
                            if oc >= 1:
                                for q in range(8):
                                    V.c.wait(t, ev_m[(oc - 1, q)])
                        qrs[ii % 2].wait(t)
                        for q in range(8):
                            ins = t.matmul(dps[q][:],
                                           qhr[ii % 2][:, q * 128:
                                                        (q + 1) * 128],
                                           dwt[:, ii * 512:(ii + 1) * 512],
                                           start=(ii == 0),
                                           stop=(ii == 63))
                        dn_m[(oc, ii)] = TE.c.n + 1
                        TE.op(ins)

                for q in range(8):
                    @blk.sync
                    def _(sy, oc=oc, q=q):
                        k_ = oc * 8 + q
                        if k_ >= 2:
                            V.c.wait(sy, ev_m[divmod(k_ - 2, 8)])
                        rls[q % 2].begin(sy)
                        rls[q % 2].dma(sy.dma_start(
                            out=rsl[q % 2][:],
                            in_=resd[q * 128:(q + 1) * 128,
                                     oc * 512:(oc + 1) * 512]))

                    @blk.vector
                    def _(v, oc=oc, q=q):
                        TE.c.wait(v, dn_m[(oc, 63)])
                        rls[q % 2].wait(v)
                        ous[q % 2].wait(v)
                        V.pre()
                        V.op(v.tensor_tensor(ot[q % 2][:], rsl[q % 2][:],
                                             dps[q][:], ALU.add))
                        ev_m[(oc, q)] = V.c.n

                    @blk.gpsimd
                    def _(g, oc=oc, q=q):
                        V.c.wait(g, ev_m[(oc, q)])
                        ous[q % 2].begin(g)
                        ous[q % 2].dma(g.dma_start(
                            out=out_q[q * 128:(q + 1) * 128,
                                      oc * 512:(oc + 1) * 512],
                            in_=ot[q % 2][:]))

    fin()
    es.close()
    return nc


def _row_map():
    rows = []
    for c in range(NCORE):
        lo = np.arange(512 * c, 512 * c + 512)
        hi = np.arange(T - 512 * (c + 1), T - 512 * c)
        rows.append(np.concatenate([lo, hi]))
    return rows


def kernel(L0_attn_x, L0_attn_Wq, L0_attn_Wk, L0_attn_Wv, L0_attn_Wo,
           L0_ffn_gate_w, L0_ffn_up_w, L0_ffn_down_w):
    x = np.ascontiguousarray(np.asarray(L0_attn_x, np.float32).reshape(T, D))
    rows = _row_map()
    nc = build_nc()
    in_maps = []
    for c in range(NCORE):
        in_maps.append({
            "x_full": x,
            "x_q": np.ascontiguousarray(x[rows[c]]),
            "rq": rows[c].astype(np.float32),
            "wqd": np.asarray(L0_attn_Wq, np.float32),
            "wkd": np.asarray(L0_attn_Wk, np.float32),
            "wvd": np.asarray(L0_attn_Wv, np.float32),
            "wod": np.asarray(L0_attn_Wo, np.float32),
            "gwd": np.asarray(L0_ffn_gate_w, np.float32),
            "uwd": np.asarray(L0_ffn_up_w, np.float32),
            "dwd": np.asarray(L0_ffn_down_w, np.float32),
        })
    res = run_bass_kernel_spmd(nc, in_maps, core_ids=list(range(NCORE)))
    out = np.zeros((T, D), np.float32)
    for c in range(NCORE):
        out[rows[c]] = res.results[c]["out_q"]
    return out.reshape(1, T, D)
